# version 01 (final)
# baseline (speedup 1.0000x reference)
import numpy as np
import ml_dtypes
import concourse.bass as bass
import concourse.mybir as mybir
from concourse.bass_utils import run_bass_kernel_spmd

F32 = mybir.dt.float32
BF16 = mybir.dt.bfloat16
FP8 = mybir.dt.float8e4
AF = mybir.ActivationFunctionType
ALU = mybir.AluOpType
AX = mybir.AxisListType

D = 1024
NH = 16
HD = 64
EPS = 1e-5
NCORE = 8
NSLOT = 6
MAX_OPS = None
SAFE_POINTS = []
DEBUG_DUMP = []

_bf = ml_dtypes.bfloat16
_f8 = ml_dtypes.float8_e4m3


def _host_consts(nch):
    ntok = nch * 128
    inv_freq = 10000.0 ** (-(np.arange(32, dtype=np.float64) * 2.0) / 64.0)
    pos = np.arange(ntok, dtype=np.float64)
    ang = pos[:, None] * inv_freq[None, :]
    cos = np.cos(ang)
    sin = np.sin(ang)
    tcc = np.zeros((128, nch * 64), dtype=np.float64)
    tsp = np.zeros((128, nch * 64), dtype=np.float64)
    for c in range(nch):
        sl = slice(c * 128, (c + 1) * 128)
        tcc[:, 64 * c:64 * c + 32] = cos[sl]
        tcc[:, 64 * c + 32:64 * c + 64] = cos[sl]
        tsp[:, 64 * c:64 * c + 32] = -sin[sl]
        tsp[:, 64 * c + 32:64 * c + 64] = sin[sl]
    tcc = tcc.astype(np.float32).astype(_bf)
    tsp = tsp.astype(np.float32).astype(_bf)
    m1 = np.triu(np.ones((128, 128), dtype=np.float32))
    mask4 = np.tile(m1, (1, 4)).astype(_bf)
    idb = np.eye(128, dtype=_bf)
    id8 = np.eye(128, dtype=_f8)
    ones_b = np.ones((1, 128), dtype=_bf)
    return dict(tcc=tcc, tsp=tsp, mask4=mask4, idb=idb, id8=id8, ones_b=ones_b)


class _Sem:
    __slots__ = ("h", "n")

    def __init__(self, h):
        self.h = h
        self.n = 0


class _Bank:

    def __init__(self, mksem, name):
        self._mksem = mksem
        self._name = name
        self.per_eng = {}

    def _get(self, eng):
        if eng not in self.per_eng:
            self.per_eng[eng] = [self._mksem(f"{self._name}_{eng}"), 0]
        return self.per_eng[eng]

    def use_waits(self):
        return [(se, n) for se, n in self.per_eng.values() if n > 0]

    def evac_inc(self, eng):
        rec = self._get(eng)
        rec[1] += 1
        return (rec[0], 1)

    def count(self, eng):
        return self._get(eng)[1]


def build_nc(nch):
    ntok = nch * 128
    nc = bass.Bass("TRN2", target_bir_lowering=False, dynamic_dma_scratch_size=8192,
                   detect_race_conditions=False)

    x_h = nc.dram_tensor("x", [ntok, D], F32, kind="ExternalInput").ap()
    wq_h = nc.dram_tensor("wq", [128, 8 * D], F32, kind="ExternalInput").ap()
    wk_h = nc.dram_tensor("wk", [128, 8 * D], F32, kind="ExternalInput").ap()
    wv_h = nc.dram_tensor("wv", [128, 8 * D], F32, kind="ExternalInput").ap()
    wo_h = nc.dram_tensor("wo", [128, 8 * D], F32, kind="ExternalInput").ap()
    wg1_h = nc.dram_tensor("wg1", [128, 8 * HD], F32, kind="ExternalInput").ap()
    wg2_h = nc.dram_tensor("wg2", [HD, D], F32, kind="ExternalInput").ap()
    gam_h = nc.dram_tensor("gam", [128, D], F32, kind="ExternalInput").ap()
    bet_h = nc.dram_tensor("bet", [128, D], F32, kind="ExternalInput").ap()
    fwt_h = nc.dram_tensor("fwt", [128, 8 * D], BF16, kind="ExternalInput").ap()
    fbias_h = nc.dram_tensor("fbias", [1, D], F32, kind="ExternalInput").ap()
    fscale_h = nc.dram_tensor("fscale", [128, 1], F32, kind="ExternalInput").ap()
    ones_h = nc.dram_tensor("ones_b", [1, 128], BF16, kind="ExternalInput").ap()
    cvec_h = nc.dram_tensor("cvec", [128, 8], F32, kind="ExternalInput").ap()
    wmod_h = nc.dram_tensor("wmod", [128, 48 * 1024], F32,
                            kind="ExternalInput").ap()
    tcc_h = nc.dram_tensor("tcc", [128, nch * 64], BF16, kind="ExternalInput").ap()
    tsp_h = nc.dram_tensor("tsp", [128, nch * 64], BF16, kind="ExternalInput").ap()
    mask_h = nc.dram_tensor("mask4", [128, 512], BF16, kind="ExternalInput").ap()
    idb_h = nc.dram_tensor("idb", [128, 128], BF16, kind="ExternalInput").ap()
    id8_h = nc.dram_tensor("id8", [128, 128], FP8, kind="ExternalInput").ap()
    out_h = nc.dram_tensor("out", [ntok, D], F32, kind="ExternalOutput").ap()

    def a(name, shape, dt):
        return nc.alloc_sbuf_tensor(name, shape, dt).ap()

    xr = a("xr", [128, NSLOT * D], F32)
    wq_s = a("wq_s", [128, 8 * D], BF16)
    wk_s = a("wk_s", [128, 8 * D], BF16)
    wv_s = a("wv_s", [128, 8 * D], BF16)
    wo_s = a("wo_s", [128, 8 * D], BF16)
    wg1_s = a("wg1_s", [128, 8 * HD], BF16)
    wg2z = a("wg2z", [128, D], BF16)
    fwt_s = a("fwt_s", [128, 8 * D], BF16)
    tcc_s = a("tcc_s", [128, nch * 64], BF16)
    tsp_s = a("tsp_s", [128, nch * 64], BF16)
    mask_s = a("mask_s", [128, 512], BF16)
    idb_s = a("idb_s", [128, 128], BF16)
    id8_s = a("id8_s", [128, 128], FP8)
    c_fm = a("c_fm", [128, 8], F32)
    c_bf = a("c_bf", [128, 8], F32)
    mod_a = a("mod_a", [1, 3 * D], BF16)
    mod_b = a("mod_b", [1, 3 * D], BF16)
    ca1_t = a("ca1_t", [128, D], BF16)
    sa_t = a("sa_t", [128, D], BF16)
    cb1_t = a("cb1_t", [128, D], BF16)
    sb_t = a("sb_t", [128, D], BF16)
    gam_t = a("gam_t", [128, D], BF16)
    bet_t = a("bet_t", [128, D], BF16)
    gb_t = a("gb_t", [128, D], BF16)
    ga_t = a("ga_t", [128, D], BF16)
    bgb_t = a("bgb_t", [128, D], BF16)
    ws_bc = a("ws_bc", [128, 1], F32)
    ones_s = a("ones_s", [1, 128], BF16)
    h1tok = [a(f"h1tok{i}", [128, D], BF16) for i in range(2)]

    h1fm = [a(f"h1fm{i}", [128, D], BF16) for i in range(2)]
    qs = [a(f"qs{i}", [128, D], BF16) for i in range(2)]
    ks = [a(f"ks{i}", [128, D], BF16) for i in range(2)]
    qrot, krot = qs, ks
    qfm = [a(f"qfm{i}", [128, D], BF16) for i in range(2)]
    kfme = [a(f"kfme{i}", [128, D], BF16) for i in range(2)]
    kfmo = [a(f"kfmo{i}", [128, D], BF16) for i in range(2)]
    vtok = [a(f"vtok{i}", [128, D], BF16) for i in range(2)]
    esb = a("esb", [128, 16 * 128], BF16)
    wm_s = [esb[:, 0:2048].bitcast(F32), fwt_s[:, 0:2048].bitcast(F32)]
    ssbe = a("ssbe", [128, 512], BF16)
    ssbo = a("ssbo", [128, 512], BF16)
    g1f = [a(f"g1f{i}", [128, 128], BF16) for i in range(2)]
    sig = [a(f"sig{i}", [128, D], BF16) for i in range(2)]
    gated = [a(f"gated{i}", [128, D], BF16) for i in range(2)]
    t_o = [a(f"t_o{i}", [128, D], BF16) for i in range(2)]
    th1 = t_o
    olin = [a(f"olin{i}", [128, D], BF16) for i in range(2)]
    ofm = [a(f"ofm{i}", [128, D], BF16) for i in range(2)]
    t2b = [a(f"t2b{i}", [128, D], BF16) for i in range(2)]
    h2b = [a(f"h2b{i}", [128, D], BF16) for i in range(2)]
    _q8b = a("q8b", [128, D], FP8)
    q8b = [_q8b, _q8b]
    _q8fm = a("q8fm", [128, D], BF16)
    q8fm = [_q8fm, _q8fm]
    u_f = t2b
    rot_tmp = a("rot_tmp", [128, D], BF16)
    t_f = gated
    sts = a("sts", [128, 56], F32)

    def col(i, w=1):
        return sts[:, i:i + w]

    FLN1, FLNO, FLN2 = 0, 14, 28

    def s12c(base, b):
        return col(base + 2 * b, 2)

    def s1c(base, b):
        return col(base + 2 * b)

    def s2c(base, b):
        return col(base + 2 * b + 1)

    def nmt2c(base, b):
        return col(base + 4 + 2 * b, 2)

    def nmc(base, b):
        return col(base + 4 + 2 * b)

    def t2c(base, b):
        return col(base + 5 + 2 * b)

    def m2c(base, b):
        return col(base + 8 + b)

    def varc(base, b):
        return col(base + 10 + b)

    def istdc(base, b):
        return col(base + 12 + b)

    amax_c = [col(42), col(43)]
    xs_c = [col(44), col(45)]
    sout_c = [col(46), col(47)]
    inv2_c = [col(48), col(49)]
    ws2_c = col(50)
    eps_c = col(52)
    lhalf_c = col(53)
    cpair = col(54, 2)

    pW = [nc.alloc_psum_tensor("pW1", [128, 512], F32).ap(),
          nc.alloc_psum_tensor("pW2", [128, 512], F32).ap()]
    pE = [nc.alloc_psum_tensor("pEa", [128, 512], F32).ap(),
          nc.alloc_psum_tensor("pEb", [128, 512], F32).ap()]
    pO = nc.alloc_psum_tensor("pO", [128, 1024], F32).ap()
    pS = nc.alloc_psum_tensor("pS", [128, 512], F32).ap()
    pT = nc.alloc_psum_tensor("pT", [128, 1024], BF16).ap()

    ENGS = ("sync", "gpsimd", "scalar", "vector", "tensor")
    OPS = {e: [] for e in ENGS}

    ALL_SEMS = []

    def sem(name):
        s = _Sem(nc.alloc_semaphore(name))
        ALL_SEMS.append(s)
        return s

    nops = [0]

    def safe(tag):
        SAFE_POINTS.append((nops[0], tag))

    def emit(eng, fn, waits=(), incs=(), force=False):
        if not force and MAX_OPS is not None and nops[0] >= MAX_OPS:
            return
        nops[0] += 1
        OPS[eng].append((tuple(waits), fn, tuple(incs)))
        for s, n in incs:
            s.n += n

    sx = sem("sx")
    sw = sem("sw")
    s_cbf = sem("s_cbf")
    s_modk = sem("s_modk")
    s_rows = sem("s_rows")
    s_h1 = sem("s_h1")
    s_pt = sem("s_pt")
    s_wmm = sem("s_wmm")
    s_emm = sem("s_emm")
    s_omm = sem("s_omm")
    s_smm = sem("s_smm")
    s_se = sem("s_se")
    s_sil = sem("s_sil")
    s_g1e = sem("s_g1e")
    s_sig = sem("s_sig")
    s_oe = sem("s_oe")
    s_bno = sem("s_bno")
    s_bn1 = sem("s_bn1")
    s_bn2 = sem("s_bn2")
    s_gt = sem("s_gt")
    s_ist = sem("s_ist")
    s_gf = sem("s_gf")
    s_if = sem("s_if")
    s_gz = sem("s_gz")
    s_amax = sem("s_amax")
    s_q8 = sem("s_q8")
    s_q8e = sem("s_q8e")
    s_tf = sem("s_tf")
    s_t2 = sem("s_t2")
    s_xn = sem("s_xn")
    s_oln = sem("s_oln")
    s_rotq = sem("s_rotq")
    s_rotk = sem("s_rotk")
    s_to = sem("s_to")
    s_th1 = sem("s_th1")
    s_h2g = sem("s_h2g")
    s_xo = sem("s_xo")
    s_st = sem("s_st")
    s_mode = sem("s_mode")

    bW = [_Bank(sem, "bW1"), _Bank(sem, "bW2")]
    bE = [_Bank(sem, "bEa"), _Bank(sem, "bEb")]
    bT = _Bank(sem, "bT")

    def wevacs(eng=None):
        out = []
        for b in bW:
            for e2, (se, n) in b.per_eng.items():
                if n > 0 and (eng is None or e2 == eng):
                    out.append((se, n))
        return out

    def tevacs():
        return [(se, n) for se, n in bT.per_eng.values() if n > 0]

    wcount = {}
    xload = {}

    def dma(eng, dst, src, waits=(), name=None, slow=False, dsem=None,
            serialize=True):
        def fn(e, dst=dst, src=src, slow=slow):
            return e.dma_start(out=dst, in_=src, allow_slow_non_contiguous=slow)
        if dsem is None:
            dsem = sem("d_" + (name or f"anon{len(wcount)}"))
        w = list(waits)
        if serialize and dsem.n > 0:
            w.append((dsem, dsem.n))
        emit(eng, fn, waits=w, incs=[(dsem, 16)])
        if name is not None:
            wcount[name] = [(dsem, dsem.n)]
        return dsem

    s_xl = sem("s_xl")
    s_cs = sem("s_cs")
    for i in range(min(2, nch)):
        dma("sync", xr[:, D * i:D * (i + 1)],
            x_h[128 * i:128 * (i + 1), :], dsem=s_xl)
        xload[i] = (s_xl, s_xl.n)
    dma("sync", tcc_s[:, :], tcc_h[:, :], dsem=s_cs)
    dma("sync", tsp_s[:, :], tsp_h[:, :], dsem=s_cs)
    dma("sync", mask_s[:, :], mask_h[:, :], dsem=s_cs)
    dma("sync", idb_s[:, :], idb_h[:, :], dsem=s_cs)
    dma("sync", id8_s[:, :], id8_h[:, :], dsem=s_cs)
    dma("sync", c_fm[:, :], cvec_h[:, :], dsem=s_cs)
    dma("sync", ones_s[:, :], ones_h[:, :], dsem=s_cs)
    dma("sync", ws_bc[:, :], fscale_h[:, :], name="ws_bc", dsem=s_cs)
    SXC = (s_cs, s_cs.n)

    def f_gz(e):
        e.memset(eps_c, EPS)
        e.memset(lhalf_c, float(np.log(0.5)))
        e.memset(cpair[:, 0:1], -1.0 / D)
        e.memset(cpair[:, 1:2], 1.0 / D)
        return e.tensor_scalar(ws2_c, ws_bc[:, :], 2.0, 0.0, ALU.mult, ALU.add)
    emit("gpsimd", f_gz, waits=wcount["ws_bc"], incs=[(s_gz, 1)])
    s_z = sem("s_z")

    def f_zero(e):
        e.memset(wg2z[64:128, :], 0.0)
        for i in range(2):
            e.memset(kfme[i][64:128, :], 0.0)
            e.memset(kfmo[i][0:64, :], 0.0)
            e.memset(g1f[i][64:128, :], 0.0)
        e.memset(ssbe[64:128, :], 0.0)
        return e.memset(ssbo[0:64, :], 0.0)
    emit("vector", f_zero, incs=[(s_z, 1)])
    emit("scalar", lambda e: e.activation(c_bf[:, :], c_fm[:, :], AF.Silu),
         waits=[SXC], incs=[(s_cbf, 1)])

    s_bct = sem("s_bct")

    def bcast(name, dst, src_row, waits=()):
        for h in range(2):
            bk = wbank()
            def f_mm(e, bk=bk, src_row=src_row, h=h):
                return e.matmul(pW[bk][:, 0:512], ones_s[:, :],
                                src_row[:, 512 * h:512 * (h + 1)],
                                start=True, stop=True)
            emit("tensor", f_mm,
                 waits=list(waits) + bW[bk].use_waits() + [SXC],
                 incs=[(s_wmm, 1)])
            def f_ev(e, bk=bk, dst=dst, h=h):
                return e.activation(dst[:, 512 * h:512 * (h + 1)],
                                    pW[bk][:, :], AF.Copy)
            emit("scalar", f_ev, waits=[(s_wmm, s_wmm.n)],
                 incs=[bW[bk].evac_inc("scalar")])
        wcount[name] = wevacs("scalar")

    wuse = [0]

    def wbank():
        b = wuse[0] % 2
        wuse[0] += 1
        return b

    SA, CA, GA = mod_a[0:1, 0:D], mod_a[0:1, D:2 * D], mod_a[0:1, 2 * D:3 * D]
    SB, CB, GB = mod_b[0:1, 0:D], mod_b[0:1, D:2 * D], mod_b[0:1, 2 * D:3 * D]

    mod_tiles = [pW[0][0:1, :], pW[1][0:1, :]]
    mod_banks = [bW[0], bW[1]]

    swm = [sem("swm0"), sem("swm1")]

    def mod_waits():
        return [(b.per_eng["scalar"][0], b.count("scalar")) for b in mod_banks
                if "scalar" in b.per_eng and b.count("scalar") > 0]

    def mod_round(r):
        for k in range(8):
            c0 = 1024 * (8 * r + k)
            src = wmod_h[:, c0:c0 + 1024]
            w = []
            kk = 8 * r + k
            if kk >= 2:
                w = [(s_modk, kk - 1)]
            dma("gpsimd", wm_s[k % 2][:, :], src, waits=w, dsem=swm[k % 2])
        for k in range(8):
            kk = 8 * r + k
            need = (swm[k % 2], 16 * (kk // 2 + 1))
            def mk(k=k):
                def f(e):
                    last = None
                    for t in range(2):
                        last = e.matmul(mod_tiles[t], c_bf[:, k:k + 1],
                                        wm_s[k % 2][:, 512 * t:512 * (t + 1)],
                                        start=(k == 0), stop=(k == 7))
                    return last
                return f
            w = [need, (s_cbf, 1)]
            if k == 0:
                for b in mod_banks:
                    w += b.use_waits()
            emit("tensor", mk(), waits=w, incs=[(s_modk, 1)])
        dstrow = mod_a if r < 3 else mod_b
        for t in range(2):
            coff = 1024 * (r % 3) + 512 * t
            def mk(t=t, dstrow=dstrow, coff=coff):
                def f(e):
                    return e.activation(dstrow[0:1, coff:coff + 512],
                                        mod_tiles[t], AF.Copy)
                return f
            ew = [(s_rows, s_rows.n)] if r >= 4 else []
            emit("scalar", mk(), waits=[(s_modk, 8 * (r + 1))] + ew,
                 incs=[mod_banks[t].evac_inc("scalar")])

    def add1_row(dst, waits, extra=()):
        def f(e, dst=dst):
            return e.tensor_single_scalar(dst, dst, 1.0, ALU.add)
        emit("vector", f, waits=list(waits) + list(extra), incs=[(s_rows, 1)])

    def clamp_row(src, s1, s2, waits, extra=()):
        def f(e, src=src, s1=s1, s2=s2):
            e.tensor_single_scalar(s1, src, 0.25, ALU.max)
            e.tensor_single_scalar(s2, src, -0.25, ALU.min)
            e.tensor_single_scalar(src, src, 0.0, ALU.is_ge)
            e.tensor_sub(s1, s1, s2)
            e.tensor_mul(src, src, s1)
            return e.tensor_add(src, src, s2)
        emit("vector", f, waits=list(waits) + list(extra), incs=[(s_rows, 1)])

    dma("gpsimd", gam_t[:, :], gam_h[:, :], name="gam_t")
    dma("gpsimd", bet_t[:, :], bet_h[:, :], name="bet_t")

    safe("pre-mod")
    mod_round(0)
    bcast("sa_t", sa_t[:, :], SA, waits=mod_waits())
    safe("mod0-done")
    mod_round(1)
    add1_row(CA, mod_waits())
    bcast("ca1_t", ca1_t[:, :], CA, waits=[(s_rows, s_rows.n)])
    dma("gpsimd", wq_s[:, :], wq_h[:, :], name="wq")
    dma("gpsimd", wk_s[:, :], wk_h[:, :], name="wk")
    dma("gpsimd", wv_s[:, :], wv_h[:, :], name="wv")
    safe("mod1-done")
    mod_round(2)
    clamp_row(GA, CB, GB, mod_waits())
    bcast("ga_t", ga_t[:, :], GA, waits=[(s_rows, s_rows.n)])
    safe("mod2-done")
    mod_round(3)
    bcast("sb_t", sb_t[:, :], SB, waits=mod_waits())
    dma("gpsimd", wo_s[:, :], wo_h[:, :], name="wo")

    def wo_fold(e):
        last = None
        for f_ in range(8):
            last = e.tensor_mul(wo_s[:, D * f_:D * (f_ + 1)],
                                wo_s[:, D * f_:D * (f_ + 1)], ga_t[:, :])
        return last
    emit("vector", wo_fold, waits=wcount["ga_t"] + wcount["wo"],
         incs=[(s_rows, 1)])
    WO_FOLD = s_rows.n

    safe("mod3-done")
    mod_round(4)
    add1_row(CB, mod_waits())
    bcast("cb1_t", cb1_t[:, :], CB, waits=[(s_rows, s_rows.n)])
    safe("mod4-done")
    mod_round(5)
    clamp_row(GB, SA, CA, mod_waits(),
              extra=wcount["ga_t"] + wcount["ca1_t"] + wcount["sa_t"] +
                    wcount["cb1_t"])
    bcast("gb_t", gb_t[:, :], GB, waits=[(s_rows, s_rows.n)])
    dma("gpsimd", CA, fbias_h[:, :], waits=[(s_rows, s_rows.n)], name="fbias")

    def f_bgb(e):
        return e.tensor_mul(SA, GB, CA)
    emit("vector", f_bgb, waits=wcount["fbias"] + wcount["gb_t"],
         incs=[(s_rows, 1)])
    bcast("bgb_t", bgb_t[:, :], SA, waits=[(s_rows, s_rows.n)])
    safe("mod-done")
    dma("gpsimd", wg1_s[:, :], wg1_h[:, :], name="wg1")
    dma("gpsimd", wg2z[0:64, :], wg2_h[:, :], name="wg2")
    dma("gpsimd", fwt_s[:, :], fwt_h[:, :], waits=mod_waits(), name="fwt")

    def stats_ops(e, src, base, b):
        e.reduce_sum(s1c(base, b), src, AX.X)
        e.scalar_tensor_tensor(rot_tmp[:, :], src, 1.0, src,
                               ALU.mult, ALU.mult)
        return e.reduce_sum(s2c(base, b), rot_tmp[:, :], AX.X)

    def g_tail(e, base, b):
        e.tensor_mul(nmt2c(base, b), s12c(base, b), cpair)
        e.tensor_mul(m2c(base, b), nmc(base, b), nmc(base, b))
        e.tensor_sub(varc(base, b), t2c(base, b), m2c(base, b))
        return e.tensor_scalar_max(varc(base, b), varc(base, b), 0.0)

    def act_tail(e, base, b):
        e.activation(m2c(base, b), varc(base, b), AF.Ln, bias=eps_c)
        return e.activation(istdc(base, b), m2c(base, b), AF.Exp, scale=-0.5)

    def ln1_chain(j):
        jb = j % 2
        xj = xr[:, D * (j % NSLOT):D * (j % NSLOT + 1)]

        def f(e, xj=xj, jb=jb):
            return stats_ops(e, xj, FLN1, jb)
        emit("vector", f, waits=[xload[j], (s_gt, s_gt.n), (s_rotq, s_rotq.n)],
             incs=[(s_bn1, 1)])

    def h1_chain(j):
        jb = j % 2
        xj = xr[:, D * (j % NSLOT):D * (j % NSLOT + 1)]

        def f_th1(e, xj=xj, jb=jb):
            e.tensor_scalar(th1[jb][:, :], xj, nmc(FLN1, jb), istdc(FLN1, jb),
                            ALU.add, ALU.mult)
            return e.tensor_mul(th1[jb][:, :], th1[jb][:, :], ca1_t[:, :])
        emit("vector", f_th1, waits=wcount["sa_t"] + [(s_oln, s_oln.n),
                                                      (s_ist, s_ist.n)],
             incs=[(s_th1, 1)])

        def f_h1(e, jb=jb):
            return e.tensor_add(h1tok[jb][:, :], th1[jb][:, :], sa_t[:, :])
        emit("gpsimd", f_h1, waits=[(s_th1, s_th1.n), (s_pt, s_pt.n)],
             incs=[(s_h1, 1)])

    def mm_half(lhs_fn, rhs_fn, extra_waits, kacc=8, n=512):
        bk = wbank()
        dst = pW[bk][:, 0:n]

        def f(e, dst=dst, lhs_fn=lhs_fn, rhs_fn=rhs_fn, kacc=kacc):
            last = None
            for f_ in range(kacc):
                last = e.matmul(dst, lhs_fn(f_), rhs_fn(f_),
                                start=(f_ == 0), stop=(f_ == kacc - 1))
            return last
        emit("tensor", f, waits=list(extra_waits) + bW[bk].use_waits(),
             incs=[(s_wmm, 1)])
        return bk, dst

    def evac(eng, fn, bk, extra_waits=()):
        emit(eng, fn, waits=[(s_wmm, s_wmm.n)] + list(extra_waits),
             incs=[bW[bk].evac_inc(eng)])

    def transpose_grp(src_ap, dst_ap, twaits, ewaits=(), eincs=()):
        def f_t(e, src_ap=src_ap):
            last = None
            for f_ in range(8):
                last = e.transpose(pT[:, 128 * f_:128 * (f_ + 1)],
                                   src_ap[:, 128 * f_:128 * (f_ + 1)], idb_s)
            return last
        emit("tensor", f_t, waits=list(twaits) + bT.use_waits(),
             incs=[(s_pt, 1)])

        def f_e(e, dst_ap=dst_ap):
            return e.activation(dst_ap[:, :], pT[:, :], AF.Copy)
        emit("scalar", f_e, waits=[(s_pt, s_pt.n)] + list(ewaits),
             incs=list(eincs) + [bT.evac_inc("scalar")])

    safe("setup-done")
    if nch > 0:
        ln1_chain(0)
        emit("gpsimd", lambda e: g_tail(e, FLN1, 0),
             waits=[(s_bn1, s_bn1.n), (s_gz, 1)], incs=[(s_gt, 1)])
        emit("scalar", lambda e: act_tail(e, FLN1, 0),
             waits=[(s_gt, s_gt.n)], incs=[(s_ist, 1)])
        h1_chain(0)

    def iteration(it):
        A, B, C, Dk = it, it - 1, it - 2, it - 3
        Ab, Bb, Cb, Db = A % 2, B % 2, C % 2, Dk % 2
        hasA = 0 <= A <= nch - 1
        hasB = 0 <= B <= nch - 1
        hasC = 0 <= C <= nch - 1
        hasD = 0 <= Dk <= nch - 1
        hasA1 = 0 <= A + 1 <= nch - 1
        xC = xr[:, D * (C % NSLOT):D * (C % NSLOT + 1)]
        xD = xr[:, D * (Dk % NSLOT):D * (Dk % NSLOT + 1)]

        j = it + 2
        if 2 <= j <= nch - 1:
            w = [(s_st, s_st.n)] if s_st.n > 0 else []
            dma("sync", xr[:, D * (j % NSLOT):D * (j % NSLOT + 1)],
                x_h[128 * j:128 * (j + 1), :], waits=w, dsem=s_xl)
            xload[j] = (s_xl, s_xl.n)

        safe("it-start")
        if hasA1 and A + 1 >= 1:
            ln1_chain(A + 1)

        safe("p3-pre")
        if hasD:
            def f_t2(e, Db=Db, xD=xD):
                e.tensor_scalar(t2b[Db][:, :], xD, nmc(FLN2, Db),
                                istdc(FLN2, Db), ALU.add, ALU.mult)
                return e.tensor_mul(t2b[Db][:, :], t2b[Db][:, :], cb1_t[:, :])
            emit("vector", f_t2, waits=wcount["cb1_t"] + [(s_xo, s_xo.n),
                                                          (s_ist, s_ist.n)],
                 incs=[(s_t2, 1)])

            def f_h2(e, Db=Db):
                return e.tensor_add(h2b[Db][:, :], t2b[Db][:, :], sb_t[:, :])
            emit("gpsimd", f_h2,
                 waits=[(s_t2, s_t2.n)] + wcount["sb_t"],
                 incs=[(s_h2g, 1)])

            def f_amax(e, Db=Db):
                return e.reduce_max(amax_c[Db], h2b[Db][:, :], AX.X,
                                    apply_absolute_value=True)
            emit("vector", f_amax, waits=[(s_h2g, s_h2g.n)], incs=[(s_amax, 1)])

            def f_xs(e, Db=Db):
                e.tensor_scalar(xs_c[Db], amax_c[Db], 1e-12, 1.0 / 448.0,
                                ALU.max, ALU.mult)
                return e.tensor_mul(sout_c[Db], xs_c[Db], ws2_c)
            emit("gpsimd", f_xs, waits=[(s_amax, s_amax.n), (s_gz, 1)],
                 incs=[(s_gf, 1)])

            def f_inv(e, Db=Db):
                e.activation(amax_c[Db], xs_c[Db], AF.Ln)
                return e.activation(inv2_c[Db], amax_c[Db], AF.Exp,
                                    scale=-1.0, bias=lhalf_c)
            emit("scalar", f_inv, waits=[(s_gf, s_gf.n)], incs=[(s_if, 1)])

            def f_q8(e, Db=Db):
                return e.activation(q8b[Db][:, :], h2b[Db][:, :], AF.Copy,
                                    scale=inv2_c[Db])
            emit("scalar", f_q8, waits=[(s_if, s_if.n), (s_h2g, s_h2g.n)],
                 incs=[(s_q8, 1)])

        safe("h1T")
        if hasA:
            transpose_grp(h1tok[Ab], h1fm[Ab],
                          twaits=[(s_h1, A + 1), SXC])

        if hasA:
            for wname, wt, dst, func in (
                    ("wq", wq_s, qs[Ab], AF.Silu),
                    ("wk", wk_s, ks[Ab], AF.Silu),
                    ("wv", wv_s, vtok[Ab], AF.Copy)):
                for h in range(2):
                    bk, psrc = mm_half(
                        (lambda f_, Ab=Ab: h1fm[Ab][:, 128 * f_:128 * (f_ + 1)]),
                        (lambda f_, wt=wt, h=h: wt[:, D * f_ + 512 * h:
                                                   D * f_ + 512 * (h + 1)]),
                        wcount[wname] + tevacs())

                    def f_ev(e, psrc=psrc, dst=dst, h=h, func=func):
                        return e.activation(dst[:, 512 * h:512 * (h + 1)],
                                            psrc, func)
                    evac("scalar", f_ev, bk)

        safe("qinT")
        if hasD:
            for h in range(2):
                bk = wbank()

                def f_qt(e, h=h, bk=bk, Db=Db):
                    last = None
                    for j_ in range(4):
                        f_ = 4 * h + j_
                        last = e.matmul(pW[bk][:, 128 * j_:128 * (j_ + 1)],
                                        q8b[Db][:, 128 * f_:128 * (f_ + 1)],
                                        id8_s[:, :], start=True, stop=True)
                    return last
                emit("tensor", f_qt, waits=[(s_q8, s_q8.n)] + bW[bk].use_waits(),
                     incs=[(s_wmm, 1)])

                def f_qte(e, h=h, bk=bk, Db=Db):
                    return e.activation(q8fm[Db][:, 512 * h:512 * (h + 1)],
                                        pW[bk][:, :], AF.Copy)
                evac("scalar", f_qte, bk)

        if hasD:
            for h in range(2):
                bk, psrc = mm_half(
                    (lambda f_, Db=Db: q8fm[Db][:, 128 * f_:128 * (f_ + 1)]),
                    (lambda f_, h=h: fwt_s[:, D * f_ + 512 * h:
                                           D * f_ + 512 * (h + 1)]),
                    wcount["fwt"] + wevacs("scalar"))

                def f_tf(e, psrc=psrc, h=h, Db=Db):
                    return e.scalar_tensor_tensor(
                        t_f[Db][:, 512 * h:512 * (h + 1)], psrc, sout_c[Db],
                        gb_t[:, 512 * h:512 * (h + 1)], ALU.mult, ALU.mult)
                evac("vector", f_tf, bk,
                     extra_waits=[(s_gf, s_gf.n)] + wcount["gb_t"])

            def f_u(e, Db=Db):
                return e.tensor_add(u_f[Db][:, :], t_f[Db][:, :], bgb_t[:, :])
            emit("gpsimd", f_u, waits=wevacs("vector") + wcount["bgb_t"])

            def f_xo(e, Db=Db, xD=xD):
                return e.tensor_add(xD, u_f[Db][:, :], xD)
            emit("gpsimd", f_xo, incs=[(s_xo, 1)])

            def f_od(e, Dk=Dk, xD=xD):
                return e.dma_start(out=out_h[128 * Dk:128 * (Dk + 1), :],
                                   in_=xD)
            w = [(s_xo, s_xo.n)]
            if s_st.n > 0:
                w.append((s_st, s_st.n))
            emit("sync", f_od, waits=w, incs=[(s_st, 16)])

        safe("rotary")
        if hasA:
            def rot(e, src, dst, cnk):
                sw_ap = bass.AP(src.tensor, src.offset + 32,
                                [[D, 128], [64, 16], [-32, 2], [1, 32]])
                tsp_ap = bass.AP(tsp_s.tensor, 64 * cnk,
                                 [[nch * 64, 128], [0, 16], [32, 2], [1, 32]])
                tcc_ap = bass.AP(tcc_s.tensor, 64 * cnk,
                                 [[nch * 64, 128], [0, 16], [32, 2], [1, 32]])
                tmp4 = rot_tmp.rearrange("p (h s j) -> p h s j", s=2, j=32)
                src4 = src.rearrange("p (h s j) -> p h s j", s=2, j=32)
                dst4 = dst.rearrange("p (h s j) -> p h s j", s=2, j=32)
                e.tensor_tensor(tmp4, sw_ap, tsp_ap, ALU.mult)
                e.tensor_tensor(dst4, src4, tcc_ap, ALU.mult)
                return e.tensor_add(dst[:, :], dst[:, :], rot_tmp[:, :])

            def f_rq(e, Ab=Ab, A=A):
                return rot(e, qs[Ab], qrot[Ab], A)
            emit("vector", f_rq, waits=wevacs("scalar") + [SXC],
                 incs=[(s_rotq, 1)])

            def f_rk(e, Ab=Ab, A=A):
                return rot(e, ks[Ab], krot[Ab], A)
            emit("vector", f_rk, incs=[(s_rotk, 1)])

        safe("p2b")
        if hasC:
            def f_to(e, Cb=Cb):
                e.tensor_scalar(t_o[Cb][:, :], gated[Cb][:, :], nmc(FLNO, Cb),
                                istdc(FLNO, Cb), ALU.add, ALU.mult)
                return e.tensor_mul(t_o[Cb][:, :], t_o[Cb][:, :], gam_t[:, :])
            emit("vector", f_to, waits=wcount["gam_t"] + [(s_h1, s_h1.n),
                                                          (s_ist, s_ist.n)],
                 incs=[(s_to, 1)])

            def f_oln(e, Cb=Cb):
                return e.tensor_add(olin[Cb][:, :], t_o[Cb][:, :], bet_t[:, :])
            emit("gpsimd", f_oln, waits=[(s_to, s_to.n)] + wcount["bet_t"],
                 incs=[(s_oln, 1)])

            transpose_grp(olin[Cb], ofm[Cb], twaits=[(s_oln, s_oln.n)])

            for h in range(2):
                bk, psrc = mm_half(
                    (lambda f_, Cb=Cb: ofm[Cb][:, 128 * f_:128 * (f_ + 1)]),
                    (lambda f_, h=h: wo_s[:, D * f_ + 512 * h:
                                          D * f_ + 512 * (h + 1)]),
                    [(s_rows, WO_FOLD)] + tevacs())

                def f_xn(e, psrc=psrc, h=h, xC=xC):
                    sl = xC[:, 512 * h:512 * (h + 1)]
                    return e.scalar_tensor_tensor(sl, psrc, 1.0, sl,
                                                  ALU.mult, ALU.add)
                evac("vector", f_xn, bk)

            def f_bn2(e, Cb=Cb, xC=xC):
                return stats_ops(e, xC, FLN2, Cb)
            emit("vector", f_bn2, waits=[(s_gt, s_gt.n)], incs=[(s_bn2, 1)])

        safe("qtkt")
        if hasB:
            transpose_grp(qrot[Bb], qfm[Bb], twaits=[(s_rotq, B + 1)])

            def f_kt(e, Bb=Bb):
                last = None
                for f_ in range(8):
                    last = e.transpose(pT[:, 128 * f_:128 * (f_ + 1)],
                                       krot[Bb][:, 128 * f_:128 * (f_ + 1)],
                                       idb_s)
                return last
            emit("tensor", f_kt, waits=[(s_rotk, B + 1)] + bT.use_waits(),
                 incs=[(s_pt, 1)])

            def f_kte(e, Bb=Bb):
                e.activation(kfme[Bb][0:64, :], pT[0:64, :], AF.Copy)
                return e.activation(kfmo[Bb][64:128, :], pT[64:128, :], AF.Copy)
            emit("scalar", f_kte, waits=[(s_pt, s_pt.n)],
                 incs=[bT.evac_inc("scalar")])

            first_w = wevacs("scalar") + tevacs() + [(s_oe, B), (s_se, B)]
            for g in range(4):
                def f_eg(e, g=g, Bb=Bb):
                    last = None
                    for j_ in range(4):
                        h = 4 * g + j_
                        hp = h // 2
                        kf = kfme[Bb] if h % 2 == 0 else kfmo[Bb]
                        last = e.matmul(
                            pE[g % 2][:, 128 * j_:128 * (j_ + 1)],
                            kf[:, 128 * hp:128 * (hp + 1)],
                            qfm[Bb][:, 128 * hp:128 * (hp + 1)],
                            start=True, stop=True)
                    return last
                emit("tensor", f_eg,
                     waits=bE[g % 2].use_waits() +
                           (first_w + [(s_z, 1)] if g == 0 else []),
                     incs=[(s_emm, 1)])

                def f_mg(e, g=g):
                    return e.tensor_tensor(esb[:, 512 * g:512 * (g + 1)],
                                           pE[g % 2][:, :], mask_s[:, :],
                                           ALU.mult)
                emit("vector", f_mg, waits=[(s_emm, s_emm.n)],
                     incs=[bE[g % 2].evac_inc("vector")])

                def f_og(e, g=g, Bb=Bb, B=B):
                    last = None
                    for j_ in range(4):
                        h = 4 * g + j_
                        hp = h // 2
                        bank_first = h % 8 == 0
                        bank_last_mm = h % 8 == 7
                        last = e.matmul(pO[:, 64 * h:64 * (h + 1)],
                                        esb[:, 128 * h:128 * (h + 1)],
                                        vtok[Bb][:, 64 * h:64 * (h + 1)],
                                        start=bank_first,
                                        stop=(bank_last_mm and B == 0))
                        if B > 0:
                            sb_ = ssbe if h % 2 == 0 else ssbo
                            last = e.matmul(
                                pO[:, 64 * h:64 * (h + 1)],
                                qfm[Bb][:, 128 * hp:128 * (hp + 1)],
                                sb_[:, 64 * (h // 2):64 * (h // 2) + 64],
                                start=False, stop=bank_last_mm)
                    return last
                emit("tensor", f_og,
                     waits=[(bE[g % 2].per_eng["vector"][0],
                             bE[g % 2].count("vector"))],
                     incs=([(s_omm, 1)] if g == 3 else []))

            def f_su(e, Bb=Bb, B=B):
                last = None
                for h in range(16):
                    hr = 64 * (h % 2)
                    last = e.matmul(
                        pS[hr:hr + 64, 64 * (h // 2):64 * (h // 2) + 64],
                        krot[Bb][:, 64 * h:64 * (h + 1)],
                        vtok[Bb][:, 64 * h:64 * (h + 1)],
                        start=(B == 0 and h < 2),
                        stop=(B == nch - 1 and h >= 14),
                        skip_group_check=True,
                        tile_position=(0, hr))
                return last
            emit("tensor", f_su, waits=[(s_se, B)], incs=[(s_smm, 1)])

            def f_se(e, B=B):
                e.activation(ssbe[0:64, :], pS[0:64, :], AF.Copy)
                return e.activation(ssbo[64:128, :], pS[64:128, :], AF.Copy)
            emit("scalar", f_se, waits=[(s_smm, B + 1)], incs=[(s_se, 1)])

            bk = wbank()

            def f_g1(e, bk=bk, Bb=Bb):
                last = None
                for f_ in range(8):
                    last = e.matmul(pW[bk][0:64, 0:128],
                                    wg1_s[:, 64 * f_:64 * (f_ + 1)],
                                    h1fm[Bb][:, 128 * f_:128 * (f_ + 1)],
                                    start=(f_ == 0), stop=(f_ == 7))
                return last
            emit("tensor", f_g1, waits=wcount["wg1"] + bW[bk].use_waits(),
                 incs=[(s_wmm, 1)])

            def f_g1e(e, bk=bk, Bb=Bb):
                return e.activation(g1f[Bb][0:64, :], pW[bk][0:64, 0:128],
                                    AF.Copy)
            evac("scalar", f_g1e, bk)

            for h in range(2):
                bk = wbank()

                def f_gate(e, h=h, bk=bk, Bb=Bb):
                    return e.matmul(pW[bk][:, 0:512], g1f[Bb][:, :],
                                    wg2z[:, 512 * h:512 * (h + 1)],
                                    start=True, stop=True)
                emit("tensor", f_gate,
                     waits=wevacs("scalar") + wcount["wg2"] +
                           bW[bk].use_waits(),
                     incs=[(s_wmm, 1)])

                def f_sig(e, h=h, bk=bk, Bb=Bb):
                    return e.activation(sig[Bb][:, 512 * h:512 * (h + 1)],
                                        pW[bk][:, :], AF.Sigmoid)
                evac("scalar", f_sig, bk)

            def f_gated(e, Bb=Bb, B=B):
                return e.tensor_tensor(gated[Bb][:, :], pO[:, :], sig[Bb][:, :],
                                       ALU.mult)
            emit("vector", f_gated, waits=[(s_omm, B + 1), (s_xo, s_xo.n)] + wevacs("scalar"),
                 incs=[(s_oe, 1)])

            def f_bno(e, Bb=Bb):
                return stats_ops(e, gated[Bb][:, :], FLNO, Bb)
            emit("vector", f_bno, incs=[(s_bno, 1)])

        tails = []
        if hasA1 and A + 1 >= 1:
            tails.append((FLN1, (A + 1) % 2))
        if hasB:
            tails.append((FLNO, Bb))
        if hasC:
            tails.append((FLN2, Cb))
        if tails:
            def f_gt(e, tails=tails):
                last = None
                for base, b in tails:
                    last = g_tail(e, base, b)
                return last
            emit("gpsimd", f_gt,
                 waits=[(s_bn1, s_bn1.n), (s_bno, s_bno.n), (s_bn2, s_bn2.n),
                        (s_gz, 1)],
                 incs=[(s_gt, 1)])

            def f_ist(e, tails=tails):
                last = None
                for base, b in tails:
                    last = act_tail(e, base, b)
                return last
            emit("scalar", f_ist, waits=[(s_gt, s_gt.n)], incs=[(s_ist, 1)])

        safe("h1next")
        if hasA1 and A + 1 >= 1:
            h1_chain(A + 1)

    for it in range(nch + 3):
        iteration(it)

    dbgmap = {
        "ca1_t": ca1_t, "sa_t": sa_t, "cb1_t": cb1_t, "sb_t": sb_t,
        "gb_t": gb_t, "ga_t": ga_t, "bgb_t": bgb_t, "gam_t": gam_t,
        "bet_t": bet_t, "mod_a": mod_a, "mod_b": mod_b,
        "h1tok0": h1tok[0], "h1fm0": h1fm[0], "qs0": qs[0], "ks0": ks[0],
        "vtok0": vtok[0], "qfm0": qfm[0], "kfme0": kfme[0], "kfmo0": kfmo[0],
        "esb": esb, "ssbe": ssbe, "ssbo": ssbo, "g1f0": g1f[0],
        "sig0": sig[0], "gated0": gated[0], "olin0": olin[0], "ofm0": ofm[0],
        "h2b0": h2b[0], "q8fm": q8fm[0], "t_f0": t_f[0], "wg2z": wg2z,
        "ws_bc": ws_bc, "sts": sts, "th10": th1[0], "xr": xr,
        "rot_tmp": rot_tmp,
    }
    barrier = [(s, s.n) for s in ALL_SEMS if s.n > 0]
    for nm in DEBUG_DUMP:
        ap = dbgmap[nm]
        shp = [ap.ap[0][1], ap.free_size()]
        dt_ = ap.dtype
        oh = nc.dram_tensor("dbg_" + nm, shp, dt_, kind="ExternalOutput").ap()
        def fdump(e, oh=oh, ap=ap):
            return e.dma_start(out=oh, in_=ap)
        emit("sync", fdump, waits=barrier + [(s_st, s_st.n)], incs=[(s_st, 16)],
             force=True)
    emit("sync", None, waits=[(s_st, s_st.n)], force=True)
    print(f"[build] emitted {nops[0]} ops")

    def replay(name):
        def run(e):
            for waits, fn, incs in OPS[name]:
                for s, v in waits:
                    if v > 0:
                        e.wait_ge(s.h, v)
                if fn is None:
                    continue
                inst = fn(e)
                first = True
                for s, n in incs:
                    if n > 0:
                        if first:
                            inst.then_inc(s.h, n)
                            first = False
                        else:
                            e.nop().then_inc(s.h, n)
        return run

    with nc.Block() as block:
        block.sync(replay("sync"))
        block.gpsimd(replay("gpsimd"))
        block.scalar(replay("scalar"))
        block.vector(replay("vector"))
        block.tensor(replay("tensor"))
    return nc


_CONST_CACHE = {}


def _consts(nch):
    if nch not in _CONST_CACHE:
        _CONST_CACHE[nch] = _host_consts(nch)
    return _CONST_CACHE[nch]


def make_in_maps(inputs, nch):
    cst = _consts(nch)
    ntok = nch * 128
    x = np.asarray(inputs["L0_attn_x"])
    w8 = np.asarray(inputs["L0_ffn_weight"])
    fwt = w8.T.astype(_bf)

    def blk(w):
        n = w.shape[1]
        return np.ascontiguousarray(
            w.reshape(8, 128, n).transpose(1, 0, 2).reshape(128, 8 * n))

    shared = {
        "wq": blk(np.asarray(inputs["L0_attn_Wq"], dtype=np.float32)),
        "wk": blk(np.asarray(inputs["L0_attn_Wk"], dtype=np.float32)),
        "wv": blk(np.asarray(inputs["L0_attn_Wv"], dtype=np.float32)),
        "wo": blk(np.asarray(inputs["L0_attn_Wo"], dtype=np.float32)),
        "wg1": blk(np.asarray(inputs["L0_attn_Wg1"], dtype=np.float32)),
        "wg2": np.ascontiguousarray(np.asarray(inputs["L0_attn_Wg2"], dtype=np.float32)),
        "gam": np.ascontiguousarray(np.broadcast_to(
            np.asarray(inputs["L0_attn_gamma"], dtype=np.float32).reshape(1, D), (128, D))),
        "bet": np.ascontiguousarray(np.broadcast_to(
            np.asarray(inputs["L0_attn_beta"], dtype=np.float32).reshape(1, D), (128, D))),
        "fwt": blk(fwt),
        "fbias": np.asarray(inputs["L0_ffn_bias"], dtype=np.float32).reshape(1, D),
        "fscale": np.ascontiguousarray(np.broadcast_to(np.asarray(
            inputs["L0_ffn_weight_scale"], dtype=np.float32).reshape(1, 1), (128, 1))),
        "wmod": np.ascontiguousarray(
            np.asarray(inputs["L0_wmod"], dtype=np.float32)
            .reshape(8, 128, 6, 1024).transpose(1, 2, 0, 3).reshape(128, -1)),
        "tcc": cst["tcc"], "tsp": cst["tsp"], "mask4": cst["mask4"],
        "idb": cst["idb"], "id8": cst["id8"], "ones_b": cst["ones_b"],
    }
    cmat = np.asarray(inputs["L0_c"], dtype=np.float32)
    in_maps = []
    for c in range(NCORE):
        m = dict(shared)
        m["x"] = np.ascontiguousarray(np.asarray(x[c, :ntok, :], dtype=np.float32))
        m["cvec"] = np.ascontiguousarray(cmat[c].reshape(8, 128).T)
        in_maps.append(m)
    return in_maps


def kernel(**inputs):
    nch = 16
    nc = build_nc(nch)
    in_maps = make_in_maps(inputs, nch)
    res = run_bass_kernel_spmd(nc, in_maps, core_ids=list(range(NCORE)))
    out = np.stack([np.asarray(res.results[c]["out"]) for c in range(NCORE)],
                   axis=0)
    return out.astype(np.float32, copy=False)
